# version 06 (final)
# baseline (speedup 1.0000x reference)
import numpy as np
import ml_dtypes
from contextlib import ExitStack

import os
import concourse.bass as bass
import concourse.mybir as mybir
from concourse.bass_utils import run_bass_kernel_spmd

F32 = mybir.dt.float32
BF16 = mybir.dt.bfloat16
AF = mybir.ActivationFunctionType
ALU = mybir.AluOpType
AX = mybir.AxisListType

D = 2048
T = 8192
TC = 1024
NCORE = 8
TB = 512
NTB = 16

INV_SQRT_DH = float(1.0 / np.sqrt(np.float32(128)))
BF16_1EM8 = float(np.float32(1e-8).astype(ml_dtypes.bfloat16))
EPS_LN = 1e-5
EPS_RMS = float(np.finfo(np.float32).eps)
C_CLAMP = 80.0

P2_GROUPS = [tuple(range(s, min(s + 3, 64))) for s in range(0, 64, 3)]
NG = len(P2_GROUPS)


class Ctr:
    def __init__(self, sem, dma=False):
        self.sem = sem
        self.n = 0
        self.step = 16 if dma else 1

    def inc(self, instr):
        instr.then_inc(self.sem, self.step)
        self.n += self.step
        return self.n

    def wait(self, eng, n):
        if n > 0:
            eng.wait_ge(self.sem, n)

    def bulk(self, instr, k):
        instr.then_inc(self.sem, k)
        self.n += k


def _offsets(order):
    return {tag: i + 1 for i, tag in enumerate(order)}


_STOP = int(os.environ.get("KSTOP", "99"))
_PHASES = os.environ.get("KPHASES", "")
_PHSET = set(int(x) for x in _PHASES.split(",") if x) if _PHASES else None


def _skip(ph):
    if _PHSET is not None:
        return ph not in _PHSET
    return _STOP < ph
_NOPOOL = bool(int(os.environ.get("KNOPOOL", "0")))
_NOQUANT = bool(int(os.environ.get("KNOQUANT", "0")))
_NOGU = bool(int(os.environ.get("KNOGU", "0")))
_NOH = bool(int(os.environ.get("KNOH", "0")))
_NOSILU = _NOH or bool(int(os.environ.get("KNOSILU", "0")))
_NOSQ = _NOH or bool(int(os.environ.get("KNOSQ", "0")))
_NOHM = _NOH or bool(int(os.environ.get("KNOHM", "0")))
_GUALT = bool(int(os.environ.get("KGUALT", "0")))
_NORMS = bool(int(os.environ.get("KNORMS", "0")))
_DRAIN2 = bool(int(os.environ.get("KDRAIN2", "0")))
_NODMA4 = bool(int(os.environ.get("KNODMA4", "0")))
_SILUCPY = bool(int(os.environ.get("KSILUCPY", "0")))
_QCUT = int(os.environ.get("KQCUT", "-1"))
_SEGN = int(os.environ.get("KSEGN", "-1"))
_SEGSTEP = {"gate0": 1, "silu0": 2, "up0": 3, "hm0": 4, "gate1": 5,
            "silu1": 6, "up1": 7, "hm1": 8, "sq0": 9, "sq1": 10,
            "rms0": 11, "rms1": 12}


def _segok(tag):
    return _SEGN < 0 or _SEGSTEP[tag] <= _SEGN
_SILUSB = bool(int(os.environ.get("KSILUSB", "0")))
_HMSB = bool(int(os.environ.get("KHMSB", "0")))
_HMALT = bool(int(os.environ.get("KHMALT", "0")))


def build_program(debug=False):
    nc = bass.Bass("TRN2", target_bir_lowering=False)

    xT = nc.dram_tensor("xT", [D, T], F32, kind="ExternalInput").ap()
    xres = nc.dram_tensor("xres", [TC, D], F32, kind="ExternalInput").ap()
    xresT = nc.dram_tensor("xresT", [D, TC], F32, kind="ExternalInput").ap()
    wq = nc.dram_tensor("wq", [D, 256], F32, kind="ExternalInput").ap()
    wk = nc.dram_tensor("wk", [D, 256], F32, kind="ExternalInput").ap()
    wv = nc.dram_tensor("wv", [D, 256], F32, kind="ExternalInput").ap()
    wo = nc.dram_tensor("wo", [D, D], F32, kind="ExternalInput").ap()
    bo_b = nc.dram_tensor("bo_b", [128, D], F32, kind="ExternalInput").ap()
    wguT = nc.dram_tensor("wguT", [D, 16384], F32, kind="ExternalInput").ap()
    wpT = nc.dram_tensor("wpT", [8192, D], F32, kind="ExternalInput").ap()
    c_sel2dup = nc.dram_tensor("c_sel2dup", [128, 4], BF16, kind="ExternalInput").ap()
    c_sel4 = nc.dram_tensor("c_sel4", [4, 128], BF16, kind="ExternalInput").ap()
    c_selsc = nc.dram_tensor("c_selsc", [4, 128], BF16, kind="ExternalInput").ap()
    c_halfcol = nc.dram_tensor("c_halfcol", [4, 1], F32, kind="ExternalInput").ap()
    c_ones_b = nc.dram_tensor("c_ones_b", [128, 1], BF16, kind="ExternalInput").ap()
    c_ones_f = nc.dram_tensor("c_ones_f", [128, 1], F32, kind="ExternalInput").ap()
    c_onesrow_b = nc.dram_tensor("c_onesrow_b", [1, 128], BF16, kind="ExternalInput").ap()
    c_onesrow_f = nc.dram_tensor("c_onesrow_f", [1, 128], F32, kind="ExternalInput").ap()
    c_one1_f = nc.dram_tensor("c_one1_f", [1, 1], F32, kind="ExternalInput").ap()
    c_ident16 = nc.dram_tensor("c_ident16", [128, 128], BF16, kind="ExternalInput").ap()
    c_ident32 = nc.dram_tensor("c_ident32", [128, 128], F32, kind="ExternalInput").ap()
    c_eps = nc.dram_tensor("c_eps", [128, 2], F32, kind="ExternalInput").ap()
    out_c = nc.dram_tensor("out_c", [TC, D], F32, kind="ExternalOutput").ap()

    dbg = {}
    if debug:
        for nm, shp, dt in [
            ("dbg_q0", [128, T], BF16), ("dbg_k0", [128, T], BF16),
            ("dbg_v0", [128, T], BF16), ("dbg_ctx0", [128, T], BF16),
            ("dbg_xcffn", [128, 16 * TC], BF16),
            ("dbg_h16", [128, 8 * TC], BF16),
            ("dbg_wf", [128, 4096], BF16),
        ]:
            dbg[nm] = nc.dram_tensor(nm, shp, dt, kind="ExternalOutput").ap()

    with ExitStack() as CTX:
        def mksb(ctx):
            return lambda name, shape, dtype: ctx.enter_context(nc.sbuf_tensor(name, shape, dtype))

        sb = mksb(CTX)
        sel2dup = sb("sel2dup", [128, 4], BF16)
        sel4 = sb("sel4", [4, 128], BF16)
        selsc = sb("selsc", [4, 128], BF16)
        halfcol = sb("halfcol", [4, 1], F32)
        ones_b = sb("ones_b", [128, 1], BF16)
        ones_f = sb("ones_f", [128, 1], F32)
        onesrow_b = sb("onesrow_b", [1, 128], BF16)
        onesrow_f = sb("onesrow_f", [1, 128], F32)
        one1_f = sb("one1_f", [1, 1], F32)
        ident16 = sb("ident16", [128, 128], BF16)
        ident32 = sb("ident32", [128, 128], F32)
        eps_sb = sb("eps_sb", [128, 2], F32)

        S = Ctr(CTX.enter_context(nc.semaphore("sS")))
        V = Ctr(CTX.enter_context(nc.semaphore("sV")))
        A = Ctr(CTX.enter_context(nc.semaphore("sA")))
        P = Ctr(CTX.enter_context(nc.semaphore("sP")))
        Y = Ctr(CTX.enter_context(nc.semaphore("sY")), dma=True)
        G = Ctr(CTX.enter_context(nc.semaphore("sG")), dma=True)

        def dq(name):
            return Ctr(CTX.enter_context(nc.semaphore(name)), dma=True)

        Gx = [dq("sGx0"), dq("sGx1")]
        Gwo = [dq("sGwo0"), dq("sGwo1")]
        Ggu = [dq("sGgu0"), dq("sGgu1")]
        Gwp = [dq("sGwp0"), dq("sGwp1")]
        Yo3 = [dq("sYo30"), dq("sYo31")]
        Yx = [dq("sYx0"), dq("sYx1")]
        Yrb = [dq("sYrb0"), dq("sYrb1")]
        Yout = [dq("sYout0"), dq("sYout1")]
        ALLC = [S, V, A, P, Y, G, Gx[0], Gx[1], Gwo[0], Gwo[1], Ggu[0], Ggu[1],
                Gwp[0], Gwp[1], Yo3[0], Yo3[1], Yx[0], Yx[1], Yrb[0], Yrb[1],
                Yout[0], Yout[1]]

        block = CTX.enter_context(nc.Block())

        def snap():
            return [(c, c.n) for c in ALLC]

        def bwait(eng, sn, me=None):
            for c, n in sn:
                if c is not me and n > 0:
                    eng.wait_ge(c.sem, n)

        p12 = ExitStack()
        with p12:
            sb1 = mksb(p12)
            q16 = [sb1(f"q16_{h}", [128, T], BF16) for h in range(2)]
            k16 = [sb1(f"k16_{h}", [128, T], BF16) for h in range(2)]
            v16 = [sb1(f"v16_{h}", [128, T], BF16) for h in range(2)]

            p1 = ExitStack()
            with p1:
                sb2 = mksb(p1)
                wq16 = sb2("wq16", [128, 16 * 256], BF16)
                wk16 = sb2("wk16", [128, 16 * 256], BF16)
                wv16 = sb2("wv16", [128, 16 * 256], BF16)

                @block.sync
                def _(sy):
                    if _skip(1):
                        return
                    for dst, src in [(sel2dup, c_sel2dup), (sel4, c_sel4), (selsc, c_selsc),
                                     (halfcol, c_halfcol), (ones_b, c_ones_b),
                                     (ones_f, c_ones_f), (onesrow_b, c_onesrow_b),
                                     (onesrow_f, c_onesrow_f), (one1_f, c_one1_f),
                                     (ident16, c_ident16), (ident32, c_ident32),
                                     (eps_sb, c_eps)]:
                        Y.inc(sy.dma_start(out=dst[:], in_=src[:]))

                @block.gpsimd
                def _(g):
                    if _skip(1):
                        return
                    for dst, src in [(wq16, wq), (wk16, wk), (wv16, wv)]:
                        G.inc(g.dma_start(
                            out=dst[:].rearrange("p (c j) -> p c j", j=256),
                            in_=src.rearrange("(c p) j -> p c j", p=128)))

                P0Y, P0G = Y.n, G.n
                x16buf = [sb2(f"x16_{b}", [128, 16 * TB], BF16) for b in range(2)]
                sqxc = [sb2(f"sqxc_{b}", [128, 16 * TB], BF16) for b in range(2)]
                m_b16 = [sb2(f"mb16_{b}", [128, TB], BF16) for b in range(2)]
                rstdb = [sb2(f"rstdb_{b}", [128, TB], F32) for b in range(2)]
                m_row16 = [sb2(f"mrow_{b}", [1, TB], BF16) for b in range(2)]
                r_t1 = sb2("r_t1", [1, TB], F32)
                r_msq = sb2("r_msq", [1, TB], F32)
                r_var = sb2("r_var", [1, TB], F32)
                r_sd = sb2("r_sd", [1, TB], F32)
                r_rstd = sb2("r_rstd", [1, TB], F32)
                p_sums = p1.enter_context(nc.psum_tensor("p_sums", [1, TB], F32))
                p_sumsq = p1.enter_context(nc.psum_tensor("p_sumsq", [1, TB], F32))
                p_mb = p1.enter_context(nc.psum_tensor("p_mb", [128, TB], F32))
                p_rb = p1.enter_context(nc.psum_tensor("p_rb", [128, TB], F32))
                p_qk = [p1.enter_context(nc.psum_tensor(f"p_qk{i}", [128, TB], F32)) for i in range(2)]
                p_v = [p1.enter_context(nc.psum_tensor(f"p_v{i}", [128, 256], F32)) for i in range(2)]

                S0, V0, A0, G0 = S.n, V.n, A.n, G.n
                SR, VR, AR = 12, 14, 2

                @block.gpsimd
                def _(g):
                    if _skip(1):
                        return
                    for tb in range(NTB):
                        b = tb & 1
                        if tb >= 2:
                            S.wait(g, S0 + SR * (tb - 2) + 12)
                        Gx[b].inc(g.dma_start(
                            out=x16buf[b][:].rearrange("p (c t) -> p c t", t=TB),
                            in_=xT[:, TB * tb:TB * (tb + 1)].rearrange("(c p) t -> p c t", p=128)))

                @block.scalar
                def _(s):
                    if _skip(1):
                        return
                    Y.wait(s, P0Y)
                    for tb in range(NTB):
                        b = tb & 1
                        Gx[b].wait(s, 16 * (tb // 2 + 1))
                        if tb >= 2:
                            V.wait(s, V0 + VR * (tb - 2) + 6)
                        A.inc(s.activation(sqxc[b][:], x16buf[b][:], AF.Square))
                        V.wait(s, V0 + VR * tb + 2)
                        if tb >= 1:
                            V.wait(s, V0 + VR * (tb - 1) + 3)
                        A.inc(s.activation(r_sd[:], r_var[:], AF.Sqrt, bias=eps_sb[0:1, 0:1]))

                @block.tensor
                def _(t):
                    if _skip(1):
                        return
                    Y.wait(t, P0Y)
                    G.wait(t, P0G)
                    for tb in range(NTB):
                        b = tb & 1
                        Gx[b].wait(t, 16 * (tb // 2 + 1))
                        if tb >= 1:
                            V.wait(t, V0 + VR * (tb - 1) + 1)
                        for dc in range(16):
                            mm = t.matmul(p_sums[0:1, :], ones_b[:, 0:1],
                                          x16buf[b][:, TB * dc:TB * (dc + 1)],
                                          start=(dc == 0), stop=(dc == 15))
                        S.inc(mm)
                        A.wait(t, A0 + AR * tb + 1)
                        if tb >= 1:
                            V.wait(t, V0 + VR * (tb - 1) + 2)
                        for dc in range(16):
                            mm = t.matmul(p_sumsq[0:1, :], ones_b[:, 0:1],
                                          sqxc[b][:, TB * dc:TB * (dc + 1)],
                                          start=(dc == 0), stop=(dc == 15))
                        S.inc(mm)
                        V.wait(t, V0 + VR * tb + 1)
                        if tb >= 1:
                            V.wait(t, V0 + VR * (tb - 1) + 4)
                        S.inc(t.matmul(p_mb[:], onesrow_b[:1, :], m_row16[b][:1, :],
                                       start=True, stop=True))
                        V.wait(t, V0 + VR * tb + 3)
                        if tb >= 1:
                            V.wait(t, V0 + VR * (tb - 1) + 5)
                        S.inc(t.matmul(p_rb[:], onesrow_f[:1, :], r_rstd[:1, :],
                                       start=True, stop=True))
                        V.wait(t, V0 + VR * tb + 6)
                        for j, (w16, h) in enumerate([(wq16, 0), (wk16, 0), (wq16, 1), (wk16, 1)]):
                            if j >= 2:
                                V.wait(t, V0 + VR * tb + 7 + (j - 2))
                            elif tb >= 1:
                                V.wait(t, V0 + VR * (tb - 1) + 9 + j)
                            for dc in range(16):
                                mm = t.matmul(p_qk[j % 2][:],
                                              w16[:, 256 * dc + 128 * h:256 * dc + 128 * (h + 1)],
                                              x16buf[b][:, TB * dc:TB * (dc + 1)],
                                              start=(dc == 0), stop=(dc == 15))
                            S.inc(mm)
                        for ts4 in range(4):
                            if ts4 >= 2:
                                V.wait(t, V0 + VR * tb + 11 + (ts4 - 2))
                            elif tb >= 1:
                                V.wait(t, V0 + VR * (tb - 1) + 13 + ts4)
                            for dc in range(16):
                                mm = t.matmul(p_v[ts4 % 2][:],
                                              x16buf[b][:, TB * dc + 128 * ts4:TB * dc + 128 * (ts4 + 1)],
                                              wv16[:, 256 * dc:256 * (dc + 1)],
                                              start=(dc == 0), stop=(dc == 15))
                            S.inc(mm)

                @block.vector
                def _(v):
                    if _skip(1):
                        return
                    for tb in range(NTB):
                        b = tb & 1
                        S.wait(v, S0 + SR * tb + 1)
                        V.inc(v.tensor_scalar(m_row16[b][:], p_sums[0:1, :], 1.0 / D, None, ALU.mult))
                        S.wait(v, S0 + SR * tb + 2)
                        if tb >= 1:
                            A.wait(v, A0 + AR * (tb - 1) + 2)
                        v.tensor_scalar(r_t1[:], p_sumsq[0:1, :], 1.0 / D, None, ALU.mult)
                        v.tensor_tensor(r_msq[:], m_row16[b][:1, :], m_row16[b][:1, :], ALU.mult)
                        V.inc(v.tensor_sub(r_var[:], r_t1[:], r_msq[:]))
                        A.wait(v, A0 + AR * tb + 2)
                        if tb >= 1:
                            S.wait(v, S0 + SR * (tb - 1) + 4)
                        V.inc(v.reciprocal(r_rstd[:], r_sd[:]))
                        S.wait(v, S0 + SR * tb + 3)
                        V.inc(v.tensor_copy(m_b16[b][:], p_mb[:]))
                        S.wait(v, S0 + SR * tb + 4)
                        V.inc(v.tensor_copy(rstdb[b][:], p_rb[:]))
                        for dc in range(16):
                            v.tensor_sub(sqxc[b][:, TB * dc:TB * (dc + 1)],
                                         x16buf[b][:, TB * dc:TB * (dc + 1)], m_b16[b][:])
                        for dc in range(16):
                            mm = v.tensor_tensor(x16buf[b][:, TB * dc:TB * (dc + 1)],
                                                 sqxc[b][:, TB * dc:TB * (dc + 1)], rstdb[b][:],
                                                 ALU.mult)
                        V.inc(mm)
                        for j, (dst, h) in enumerate([(q16, 0), (k16, 0), (q16, 1), (k16, 1)]):
                            S.wait(v, S0 + SR * tb + 5 + j)
                            V.inc(v.tensor_copy(dst[h][:, TB * tb:TB * (tb + 1)], p_qk[j % 2][:]))
                        for ts4 in range(4):
                            S.wait(v, S0 + SR * tb + 9 + ts4)
                            blk = 4 * tb + ts4
                            v.tensor_copy(v16[0][:, 128 * blk:128 * (blk + 1)], p_v[ts4 % 2][:, 0:128])
                            V.inc(v.tensor_copy(v16[1][:, 128 * blk:128 * (blk + 1)],
                                                p_v[ts4 % 2][:, 128:256]))

                assert _skip(1) or (S.n == S0 + SR * NTB)
                assert _skip(1) or (V.n == V0 + VR * NTB)
                assert _skip(1) or (A.n == A0 + AR * NTB)

            bar1 = snap()
            ctx_tm = [sb1(f"ctxtm_{h}", [128, T], BF16) for h in range(2)]
            p2 = ExitStack()
            with p2:
                sb3 = mksb(p2)
                sq_s = sb3("sq_s", [128, T], BF16)
                E_gb = [sb3(f"egb{i}", [128, 1536], BF16) for i in range(2)]
                E_acc = sb3("eacc", [128, 512], BF16)
                ctxT16 = [sb3(f"ctxT{h}", [128, T], BF16) for h in range(2)]
                recip_all = [sb3(f"recip{h}", [128, 64], F32) for h in range(2)]
                bias_col = [sb3(f"bias{h}", [128, 1], F32) for h in range(2)]
                r_red = sb3("r_red", [128, 1], F32)
                r_red2 = sb3("r_red2", [128, 1], F32)
                r_sc = sb3("r_sc", [1, 4], F32)
                r_qk2 = sb3("r_qk2", [1, 2], F32)
                recip_row = sb3("recip_row", [1, 512], F32)

                p2p = ExitStack()
                with p2p:
                    stA = p2p.enter_context(nc.psum_tensor("stA", [128, 1536], F32))
                    stB = p2p.enter_context(nc.psum_tensor("stB", [128, 1536], F32))
                    pv = p2p.enter_context(nc.psum_tensor("pv", [128, 512], F32))
                    misc = p2p.enter_context(nc.psum_tensor("misc", [128, 512], F32))

                    S0, V0, A0 = S.n, V.n, A.n

                    @block.scalar
                    def _(s):
                        if _skip(2):
                            return
                        bwait(s, bar1, A)
                        for h in range(2):
                            if h == 1:
                                V.wait(s, V0 + 2)
                            A.inc(s.activation(sq_s[:], q16[h][:], AF.Square))
                            V.wait(s, V0 + 5 * h + 1)
                            A.inc(s.activation(sq_s[:], k16[h][:], AF.Square))
                            V.wait(s, V0 + 5 * h + 3)
                            A.inc(s.activation(r_sc[:1, 1:2], r_sc[:1, 0:1], AF.Ln))
                            A.inc(s.activation(r_sc[:1, 2:3], r_sc[:1, 1:2], AF.Exp, scale=0.5))

                    @block.vector
                    def _(v):
                        if _skip(2):
                            return
                        bwait(v, bar1, V)
                        for h in range(2):
                            A.wait(v, A0 + 4 * h + 1)
                            if h == 1:
                                S.wait(v, S0 + 2)
                            V.inc(v.reduce_max(r_red[:], sq_s[:], axis=AX.X))
                            A.wait(v, A0 + 4 * h + 2)
                            V.inc(v.reduce_max(r_red2[:], sq_s[:], axis=AX.X))
                            S.wait(v, S0 + 3 * h + 2)
                            if h == 1:
                                A.wait(v, A0 + 3)
                            v.tensor_copy(r_qk2[:], misc[0:1, 0:2])
                            V.inc(v.tensor_tensor(r_sc[:1, 0:1], r_qk2[:1, 0:1],
                                                  r_qk2[:1, 1:2], ALU.mult))
                            A.wait(v, A0 + 4 * h + 4)
                            V.inc(v.tensor_scalar(r_sc[:1, 3:4], r_sc[:1, 2:3],
                                                  -INV_SQRT_DH, -C_CLAMP, ALU.mult, ALU.max))
                            S.wait(v, S0 + 3 * h + 3)
                            V.inc(v.tensor_copy(bias_col[h][:], misc[:, 4:5]))

                    @block.tensor
                    def _(t):
                        if _skip(2):
                            return
                        bwait(t, bar1, S)
                        for h in range(2):
                            V.wait(t, V0 + 5 * h + 1)
                            S.inc(t.matmul(misc[0:1, 0:1], r_red[:], ones_f[:, 0:1],
                                           start=True, stop=True))
                            V.wait(t, V0 + 5 * h + 2)
                            S.inc(t.matmul(misc[0:1, 1:2], r_red2[:], ones_f[:, 0:1],
                                           start=True, stop=True))
                            V.wait(t, V0 + 5 * h + 4)
                            S.inc(t.matmul(misc[:, 4:5], onesrow_f[:1, :], r_sc[:1, 3:4],
                                           start=True, stop=True))

                    assert _skip(2) or (S.n == S0 + 6 and V.n == V0 + 10 and A.n == A0 + 8)

                    S0, V0, A0 = S.n, V.n, A.n
                    SRm, VRm, ARm = 46, 25, 22

                    def st_bank(g):
                        return stA if (g % 2 == 0) else stB

                    @block.tensor
                    def _(t):
                        if _skip(3):
                            return
                        for idx in range(32):
                            h, qb = idx >> 4, idx & 15
                            qs = slice(512 * qb, 512 * (qb + 1))
                            for g, kts in enumerate(P2_GROUPS):
                                if g >= 2:
                                    A.wait(t, A0 + ARm * idx + g - 1)
                                elif idx > 0:
                                    A.wait(t, A0 + ARm * (idx - 1) + 21 + g)
                                for i, kt in enumerate(kts):
                                    mm = t.matmul(st_bank(g)[:, 512 * i:512 * (i + 1)],
                                                  k16[h][:, 128 * kt:128 * (kt + 1)],
                                                  q16[h][:, qs], start=True, stop=True,
                                                  skip_group_check=True)
                                S.inc(mm)
                                if g >= 1:
                                    gp = g - 1
                                    A.wait(t, A0 + ARm * idx + gp + 1)
                                    if gp == 0 and idx > 0:
                                        V.wait(t, V0 + VRm * (idx - 1) + 23)
                                    for i, kt in enumerate(P2_GROUPS[gp]):
                                        mm = t.matmul(pv[:], v16[h][:, 128 * kt:128 * (kt + 1)],
                                                      E_gb[gp % 2][:, 512 * i:512 * (i + 1)],
                                                      start=(kt == 0), stop=(kt == 63),
                                                      skip_group_check=True)
                                    S.inc(mm)
                            gp = NG - 1
                            A.wait(t, A0 + ARm * idx + gp + 1)
                            for i, kt in enumerate(P2_GROUPS[gp]):
                                mm = t.matmul(pv[:], v16[h][:, 128 * kt:128 * (kt + 1)],
                                              E_gb[gp % 2][:, 512 * i:512 * (i + 1)],
                                              start=(kt == 0), stop=(kt == 63),
                                              skip_group_check=True)
                            S.inc(mm)
                            V.wait(t, V0 + VRm * idx + 22)
                            if idx > 0:
                                V.wait(t, V0 + VRm * (idx - 1) + 25)
                            S.inc(t.matmul(misc[0:1, 0:512], ones_b[:, 0:1], E_acc[:],
                                           start=True, stop=True))
                            V.wait(t, V0 + VRm * idx + 24)
                            for j in range(4):
                                mm = t.matmul(misc[:, j:j + 1],
                                              recip_row[:1, 128 * j:128 * (j + 1)],
                                              one1_f[:1, :], start=True, stop=True)
                            S.inc(mm)

                    @block.scalar
                    def _(s):
                        if _skip(3):
                            return
                        for idx in range(32):
                            h = idx >> 4
                            for g, kts in enumerate(P2_GROUPS):
                                S.wait(s, S0 + SRm * idx + (2 * g if g >= 1 else 1))
                                if g >= 2:
                                    V.wait(s, V0 + VRm * idx + g - 1)
                                elif idx > 0:
                                    V.wait(s, V0 + VRm * (idx - 1) + 21 + g)
                                A.inc(s.activation(E_gb[g % 2][:, 0:512 * len(kts)],
                                                   st_bank(g)[:, 0:512 * len(kts)], AF.Exp,
                                                   bias=bias_col[h][:], scale=INV_SQRT_DH))

                    @block.vector
                    def _(v):
                        if _skip(3):
                            return
                        for idx in range(32):
                            h, qb = idx >> 4, idx & 15
                            for g, kts in enumerate(P2_GROUPS):
                                A.wait(v, A0 + ARm * idx + g + 1)
                                if g == 0 and idx > 0:
                                    S.wait(v, S0 + SRm * (idx - 1) + 45)
                                for i, kt in enumerate(kts):
                                    if g == 0 and i == 0:
                                        mm = v.tensor_copy(E_acc[:], E_gb[0][:, 0:512])
                                    else:
                                        mm = v.tensor_tensor(E_acc[:], E_acc[:],
                                                             E_gb[g % 2][:, 512 * i:512 * (i + 1)],
                                                             ALU.add)
                                V.inc(mm)
                            S.wait(v, S0 + SRm * idx + 44)
                            V.inc(v.tensor_copy(ctxT16[h][:, 512 * qb:512 * (qb + 1)], pv[:]))
                            S.wait(v, S0 + SRm * idx + 45)
                            V.inc(v.reciprocal(recip_row[:], misc[0:1, 0:512]))
                            S.wait(v, S0 + SRm * idx + 46)
                            V.inc(v.tensor_copy(recip_all[h][:, 4 * qb:4 * qb + 4], misc[:, 0:4]))

                    assert _skip(3) or (S.n == S0 + SRm * 32)
                    assert _skip(3) or (V.n == V0 + VRm * 32)
                    assert _skip(3) or (A.n == A0 + ARm * 32)

                bar2 = snap()
                with nc.psum_tensor("tp_a", [128, 256], BF16) as tp_a, \
                        nc.psum_tensor("tp_b", [128, 256], BF16) as tp_b:
                    tp2 = [tp_a, tp_b]
                    S0, V0 = S.n, V.n

                    @block.tensor
                    def _(t):
                        if _skip(4):
                            return
                        bwait(t, bar2, S)
                        for i in range(128):
                            h, tk = i >> 6, i & 63
                            p, sl = i // 2, i % 2
                            if p >= 2:
                                V.wait(t, V0 + 2 * (p - 2) + 2)
                            S.inc(t.transpose(tp2[p % 2][:, 128 * sl:128 * (sl + 1)],
                                              ctxT16[h][:, 128 * tk:128 * (tk + 1)], ident16[:]))

                    @block.vector
                    def _(v):
                        if _skip(4):
                            return
                        bwait(v, bar2, V)
                        for i in range(128):
                            h, tk = i >> 6, i & 63
                            p, sl = i // 2, i % 2
                            S.wait(v, S0 + 2 * p + 2)
                            V.inc(v.tensor_scalar(ctx_tm[h][:, 128 * tk:128 * (tk + 1)],
                                                  tp2[p % 2][:, 128 * sl:128 * (sl + 1)],
                                                  recip_all[h][:, tk:tk + 1], None, ALU.mult))

                    assert _skip(4) or (S.n == S0 + 128 and V.n == V0 + 128)

                if debug:
                    bard = snap()

                    @block.sync
                    def _(sy):
                        if _skip(4):
                            return
                        bwait(sy, bard)
                        Y.inc(sy.dma_start(out=dbg["dbg_q0"], in_=q16[0][:]))
                        Y.inc(sy.dma_start(out=dbg["dbg_k0"], in_=k16[0][:]))
                        Y.inc(sy.dma_start(out=dbg["dbg_v0"], in_=v16[0][:]))
                        Y.inc(sy.dma_start(out=dbg["dbg_ctx0"], in_=ctx_tm[0][:]))

            bar3 = snap()
            p3 = ExitStack()
            with p3:
                sb4 = mksb(p3)
                wo16 = [sb4(f"wo16_{i}", [128, 16 * 512], BF16) for i in range(2)]
                bo_sb = sb4("bo_sb", [128, D], F32)
                stage3 = [sb4(f"stage3{i}", [128, 512], F32) for i in range(2)]
                p_o = [p3.enter_context(nc.psum_tensor(f"p_o{i}", [128, 512], F32)) for i in range(2)]
                S0, V0, G0, Y0 = S.n, V.n, G.n, Y.n
                outv = out_c.rearrange("(h2 d r4) j -> h2 r4 d j", h2=2, r4=4)

                @block.sync
                def _(sy):
                    if _skip(5):
                        return
                    bwait(sy, bar3, Y)
                    Y.inc(sy.dma_start(out=bo_sb[:], in_=bo_b[:]))
                    for i in range(32):
                        jb, pr = i >> 3, i & 7
                        h, r = pr >> 2, pr & 3
                        V.wait(sy, V0 + i + 1)
                        Y.inc(sy.dma_start(out=outv[h, r, :, 512 * jb:512 * (jb + 1)],
                                           in_=stage3[i % 2][:]))

                @block.gpsimd
                def _(g):
                    if _skip(5):
                        return
                    bwait(g, bar3, G)
                    for jb in range(4):
                        if jb >= 2:
                            S.wait(g, S0 + 8 * (jb - 1))
                        Gwo[jb % 2].inc(g.dma_start(
                            out=wo16[jb % 2][:].rearrange("p (c j) -> p c j", j=512),
                            in_=wo[:, 512 * jb:512 * (jb + 1)].rearrange("(c p) j -> p c j", p=128)))

                @block.tensor
                def _(t):
                    if _skip(5):
                        return
                    bwait(t, bar3, S)
                    for i in range(32):
                        jb, pr = i >> 3, i & 7
                        h, r = pr >> 2, pr & 3
                        Gwo[jb % 2].wait(t, 16 * (jb // 2 + 1))
                        if i >= 2:
                            V.wait(t, V0 + i - 1)
                        for ck in range(16):
                            mm = t.matmul(p_o[i % 2][:],
                                          ctx_tm[h][:, 128 * (16 * r + ck):128 * (16 * r + ck + 1)],
                                          wo16[jb % 2][:, 512 * ck:512 * (ck + 1)],
                                          start=(ck == 0), stop=(ck == 15))
                        S.inc(mm)

                @block.vector
                def _(v):
                    if _skip(5):
                        return
                    bwait(v, bar3, V)
                    Y.wait(v, Y0 + 16)
                    for i in range(32):
                        jb = i >> 3
                        S.wait(v, S0 + i + 1)
                        if i >= 2:
                            Y.wait(v, Y0 + 16 + 16 * (i - 1))
                        V.inc(v.tensor_tensor(stage3[i % 2][:], p_o[i % 2][:],
                                              bo_sb[:, 512 * jb:512 * (jb + 1)], ALU.add))

                assert _skip(5) or (S.n == S0 + 32 and V.n == V0 + 32 and Y.n == Y0 + 16 * 33)

        bar4 = snap()
        pxc = ExitStack()
        xcffn = pxc.enter_context(nc.sbuf_tensor("xcffn", [128, 16 * TC], BF16, side="right"))
        rmscols = sb("rmscols", [128, 8], F32)

        p4p = ExitStack()
        with p4p:
            sb5 = mksb(p4p)
            x16f = sb5("x16f", [128, 16 * TC], BF16)
            xsqf = sb5("xsqf", [128, 16 * TC], BF16)
            fm16 = sb5("fm16", [128, TC], BF16)
            frstd = sb5("frstd", [128, TC], F32)
            fr_m = sb5("fr_m", [1, TC], BF16)
            fr_t1 = sb5("fr_t1", [1, TC], F32)
            fr_msq = sb5("fr_msq", [1, TC], F32)
            fr_var = sb5("fr_var", [1, TC], F32)
            fr_sd = sb5("fr_sd", [1, TC], F32)
            fr_rstd = sb5("fr_rstd", [1, TC], F32)
            pf_s = p4p.enter_context(nc.psum_tensor("pf_s", [1, TC], F32))
            pf_q = p4p.enter_context(nc.psum_tensor("pf_q", [1, TC], F32))
            pf_mb = p4p.enter_context(nc.psum_tensor("pf_mb", [128, TC], F32))
            pf_rb = p4p.enter_context(nc.psum_tensor("pf_rb", [128, TC], F32))
            S0, V0, A0, G0 = S.n, V.n, A.n, G.n

            @block.gpsimd
            def _(g):
                if _skip(6):
                    return
                bwait(g, bar4, G)
                G.inc(g.dma_start(
                    out=x16f[:].rearrange("p (c t) -> p c t", t=TC),
                    in_=xresT.rearrange("(c p) t -> p c t", p=128)))

            @block.scalar
            def _(s):
                if _skip(6):
                    return
                bwait(s, bar4, A)
                G.wait(s, G0 + 16)
                A.inc(s.activation(xsqf[:], x16f[:], AF.Square))
                V.wait(s, V0 + 1)
                A.inc(s.activation(fr_sd[:], fr_var[:], AF.Sqrt, bias=eps_sb[0:1, 0:1]))

            @block.tensor
            def _(t):
                if _skip(6):
                    return
                bwait(t, bar4, S)
                G.wait(t, G0 + 16)
                for half in range(2):
                    for dc in range(16):
                        mm = t.matmul(pf_s[0:1, 512 * half:512 * (half + 1)], ones_b[:, 0:1],
                                      x16f[:, TC * dc + 512 * half:TC * dc + 512 * (half + 1)],
                                      start=(dc == 0), stop=(dc == 15))
                S.inc(mm)
                A.wait(t, A0 + 1)
                for half in range(2):
                    for dc in range(16):
                        mm = t.matmul(pf_q[0:1, 512 * half:512 * (half + 1)], ones_b[:, 0:1],
                                      xsqf[:, TC * dc + 512 * half:TC * dc + 512 * (half + 1)],
                                      start=(dc == 0), stop=(dc == 15))
                S.inc(mm)
                V.wait(t, V0 + 1)
                for half in range(2):
                    mm = t.matmul(pf_mb[:, 512 * half:512 * (half + 1)], onesrow_b[:1, :],
                                  fr_m[:1, 512 * half:512 * (half + 1)], start=True, stop=True)
                S.inc(mm)
                V.wait(t, V0 + 2)
                for half in range(2):
                    mm = t.matmul(pf_rb[:, 512 * half:512 * (half + 1)], onesrow_f[:1, :],
                                  fr_rstd[:1, 512 * half:512 * (half + 1)], start=True, stop=True)
                S.inc(mm)

            @block.vector
            def _(v):
                if _skip(6):
                    return
                bwait(v, bar4, V)
                S.wait(v, S0 + 2)
                v.tensor_scalar(fr_m[:], pf_s[0:1, :], 1.0 / D, None, ALU.mult)
                v.tensor_scalar(fr_t1[:], pf_q[0:1, :], 1.0 / D, None, ALU.mult)
                v.tensor_tensor(fr_msq[:], fr_m[:1, :], fr_m[:1, :], ALU.mult)
                V.inc(v.tensor_sub(fr_var[:], fr_t1[:], fr_msq[:]))
                A.wait(v, A0 + 2)
                V.inc(v.reciprocal(fr_rstd[:], fr_sd[:]))
                S.wait(v, S0 + 3)
                V.inc(v.tensor_copy(fm16[:], pf_mb[:]))
                S.wait(v, S0 + 4)
                V.inc(v.tensor_copy(frstd[:], pf_rb[:]))
                for dc in range(16):
                    v.tensor_sub(xsqf[:, TC * dc:TC * (dc + 1)],
                                 x16f[:, TC * dc:TC * (dc + 1)], fm16[:])
                for dc in range(16):
                    mm = v.tensor_tensor(xcffn[:, TC * dc:TC * (dc + 1)],
                                         xsqf[:, TC * dc:TC * (dc + 1)], frstd[:], ALU.mult)
                V.inc(mm)

            assert _skip(6) or (S.n == S0 + 4 and V.n == V0 + 5 and A.n == A0 + 2)

        pq = ExitStack()
        with pq:
            sb6 = mksb(pq)
            htmp = sb6("htmp", [128, 512], F32)
            hsq = [sb6(f"hsq{i}", [128, 512], BF16) for i in range(2)]
            h16 = sb6("h16", [128, 64 * TC], BF16)
            qs1 = [sb6(f"qs1_{i}", [128, 512], BF16) for i in range(2)]
            qs2 = [sb6(f"qs2_{i}", [128, 512], BF16) for i in range(2)]
            qs3 = [sb6(f"qs3_{i}", [128, 512], BF16) for i in range(2)]
            T_sb = sb6("T_sb", [128, 512], F32)
            sc_sb = sb6("sc_sb", [128, 512], BF16)
            mean4 = sb6("mean4", [4, 512], BF16)
            scl4 = sb6("scl4", [4, 512], BF16)

            def make_quant(wbufs, ntile, tw, nrd, ofs):
                (sG_, sT_, aE_, aS_, vA_, vC_, vM_, pW_, pF_, gD_, p_gs, p_t, p_s) = ofs

                def q_pe(t, c, rd):
                    V.wait(t, vA_(c, rd))
                    if _NOQUANT:
                        S.inc(t.nop())
                        V.wait(t, vC_(c, rd))
                        S.inc(t.nop())
                        return
                    if (c, rd) != (0, 0):
                        pc, prd = (c, rd - 1) if rd >= 1 else (c - 1, nrd - 1)
                        V.wait(t, vC_(pc, prd))
                        A.wait(t, aE_(pc, prd))
                    for ti in range(ntile):
                        mm = t.matmul(p_gs[:, tw * ti:tw * (ti + 1)], sel2dup[:],
                                      qs1[rd % 2][:, tw * ti:tw * (ti + 1)],
                                      start=True, stop=True, skip_group_check=True)
                    S.inc(mm)
                    V.wait(t, vC_(c, rd))
                    t.matmul(p_t[:], sel4[:], scl4[:], start=True, stop=True,
                             skip_group_check=True)
                    S.inc(t.matmul(p_s[:], selsc[:], scl4[:], start=True, stop=True,
                                   skip_group_check=True))

                def q_act(s, c, rd):
                    S.wait(s, sT_(c, rd))
                    if _NOQUANT:
                        A.inc(s.nop())
                        A.inc(s.nop())
                        return
                    if (c, rd) != (0, 0):
                        pc, prd = (c, rd - 1) if rd >= 1 else (c - 1, nrd - 1)
                        V.wait(s, vC_(pc, prd))
                        P.wait(s, pW_(pc, prd))
                    s.copy(T_sb[:], p_t[:])
                    A.inc(s.copy(sc_sb[:], p_s[:]))
                    b = c & 1
                    off = 512 * rd
                    gsem, gval = gD_(c)
                    gsem.wait(s, gval)
                    pc2, prd2 = (c, rd - 2) if rd >= 2 else (c - 1, rd + nrd - 2)
                    if pc2 >= 0:
                        P.wait(s, pF_(pc2, prd2))
                    A.inc(s.activation(qs2[rd % 2][:], wbufs[b][:, off:off + 512], AF.Sign))

                def q_dve(v, c, rd):
                    b = c & 1
                    off = 512 * rd
                    gsem, gval = gD_(c)
                    gsem.wait(v, gval)
                    if _NOQUANT:
                        V.inc(v.nop())
                        S.wait(v, sG_(c, rd))
                        V.inc(v.nop())
                        V.inc(v.nop())
                        return
                    pc2, prd2 = (c, rd - 2) if rd >= 2 else (c - 1, rd + nrd - 2)
                    if pc2 >= 0:
                        P.wait(v, pW_(pc2, prd2))
                    V.inc(v.scalar_tensor_tensor(qs1[rd % 2][:], wbufs[b][:, off:off + 512],
                                                 -1.0, wbufs[b][:, off:off + 512],
                                                 ALU.mult, ALU.max))
                    S.wait(v, sG_(c, rd))
                    if (c, rd) != (0, 0):
                        pcc, prr = (c, rd - 1) if rd >= 1 else (c - 1, nrd - 1)
                        S.wait(v, sT_(pcc, prr))
                    v.tensor_scalar(mean4[:], p_gs[:], 1.0 / 64, BF16_1EM8, ALU.mult, ALU.max)
                    V.inc(v.tensor_scalar(scl4[:], mean4[:], halfcol[:], None, ALU.mult))
                    A.wait(v, aS_(c, rd))
                    if pc2 >= 0:
                        P.wait(v, pF_(pc2, prd2))
                    v.tensor_tensor(qs3[rd % 2][:], qs1[rd % 2][:], T_sb[:], ALU.is_gt)
                    V.inc(v.tensor_tensor(qs1[rd % 2][:], qs3[rd % 2][:], qs2[rd % 2][:],
                                          ALU.mult))

                def q_pool(g, c, rd):
                    b = c & 1
                    off = 512 * rd
                    V.wait(g, vM_(c, rd))
                    if _NOPOOL:
                        P.inc(g.nop())
                        P.inc(g.nop())
                        return
                    P.inc(g.tensor_tensor(qs2[rd % 2][:], qs1[rd % 2][:], sc_sb[:], ALU.mult))
                    g.tensor_sub(qs3[rd % 2][:], qs2[rd % 2][:], wbufs[b][:, off:off + 512])
                    P.inc(g.tensor_tensor(wbufs[b][:, off:off + 512],
                                          wbufs[b][:, off:off + 512], qs3[rd % 2][:], ALU.add))

                return q_pe, q_act, q_dve, q_pool

            bar4b = snap()
            p4 = ExitStack()
            with p4:
                sb7 = mksb(p4)
                w16g = [sb7(f"w16g{i}", [128, 4096], BF16) for i in range(2)]
                rms_row = sb7("rms_row", [1, TC], F32)
                rms_sd = sb7("rms_sd", [1, TC], F32)
                p_rms = [p4.enter_context(nc.psum_tensor(f"p_rms{i}", [1, 512], F32))
                         for i in range(2)]
                p_gsum = p4.enter_context(nc.psum_tensor("p_gsum", [4, 512], F32))
                p_T = p4.enter_context(nc.psum_tensor("p_T", [128, 512], F32))
                p_sc = p4.enter_context(nc.psum_tensor("p_sc", [128, 512], F32))
                p_cols = p4.enter_context(nc.psum_tensor("p_cols", [128, 8], F32))
                p_g = p4.enter_context(nc.psum_tensor("p_g", [128, 512], F32))
                p_u = p4.enter_context(nc.psum_tensor("p_u", [128, 512], F32))

                NCH, NRD = int(os.environ.get('KNCH', '64')), 8
                S0, V0, A0, P0, G0 = S.n, V.n, A.n, P.n, G.n

                S_ORD = (["gate0", "up0", "gate1", "up1"]
                         + [f"{k}{rd}" for rd in range(NRD) for k in ("gs", "ts")]
                         + ["rms0", "rms1"])
                A_ORD = (["silu0", "silu1", "sq0", "sq1"]
                         + [f"{k}{rd}" for rd in range(NRD) for k in ("ev", "sg")])
                V_ORD = (["hm0", "hm1"]
                         + [f"{k}{rd}" for rd in range(NRD) for k in ("aw", "cm", "m")])
                P_ORD = [f"{k}{rd}" for rd in range(NRD) for k in ("wh", "wf")]
                SO, AO, VO, PO = map(_offsets, (S_ORD, A_ORD, V_ORD, P_ORD))
                SR4, AR4, VR4, PR4 = len(S_ORD), len(A_ORD), len(V_ORD), len(P_ORD)
                SP_ORD = [f"{k}{rd}" for rd in range(NRD) for k in ("gs", "ts")]
                AP_ORD = [f"{k}{rd}" for rd in range(NRD) for k in ("ev", "sg")]
                VP_ORD = [f"{k}{rd}" for rd in range(NRD) for k in ("aw", "cm", "m")]
                SPO, APO, VPO, PPO = (_offsets(SP_ORD), _offsets(AP_ORD),
                                      _offsets(VP_ORD), _offsets(P_ORD))
                SPRE, APRE, VPRE, PPRE = len(SP_ORD), len(AP_ORD), len(VP_ORD), len(P_ORD)

                def mkpos(base0, pre_ofs, npre, seg_ofs, nseg, pre_map=None):
                    def f(tag):
                        def g_(c, rd):
                            t_ = f"{tag}{rd}"
                            if c == 0:
                                return base0 + (pre_ofs[t_] if t_ in pre_ofs else seg_ofs[t_])
                            return base0 + npre + nseg * (c - 1) + seg_ofs[t_]
                        return g_
                    return f

                sPos = mkpos(S0, SPO, SPRE, SO, SR4)
                aPos = mkpos(A0, APO, APRE, AO, AR4)
                vPos = mkpos(V0, VPO, VPRE, VO, VR4)
                pPos = mkpos(P0, PPO, PPRE, PO, PR4)
                sG_, sT_ = sPos("gs"), sPos("ts")
                aE_, aS_ = aPos("ev"), aPos("sg")
                vA_, vC_, vM_ = vPos("aw"), vPos("cm"), vPos("m")
                pW_, pF_ = pPos("wh"), pPos("wf")

                def sSeg(c, tag):
                    return S0 + SPRE + SR4 * c + SO[tag]

                def aSeg(c, tag):
                    return A0 + APRE + AR4 * c + AO[tag]

                def vSeg(c, tag):
                    return V0 + VPRE + VR4 * c + VO[tag]

                def gD_(j):
                    return (Ggu[j % 2], 32 * (j // 2 + 1))

                q_pe, q_act, q_dve, q_pool = make_quant(
                    w16g, 2, 256, NRD,
                    (sG_, sT_, aE_, aS_, vA_, vC_, vM_, pW_, pF_, gD_, p_gsum, p_T, p_sc))

                def dma_wgu(g, j):
                    if j >= 2:
                        S.wait(g, sSeg(j - 2, "up1"))
                    if _NODMA4:
                        Ggu[j % 2].bulk(g.nop(), 32)
                        return
                    Ggu[j % 2].inc(g.dma_start(
                        out=w16g[j % 2][:].rearrange("p (dc u) -> p dc u", u=256)[:, :, 0:128],
                        in_=wguT[:, 128 * j:128 * (j + 1)].rearrange("(dc p) u -> p dc u", p=128)))
                    Ggu[j % 2].inc(g.dma_start(
                        out=w16g[j % 2][:].rearrange("p (dc u) -> p dc u", u=256)[:, :, 128:256],
                        in_=wguT[:, 8192 + 128 * j:8192 + 128 * (j + 1)].rearrange(
                            "(dc p) u -> p dc u", p=128)))

                @block.gpsimd
                def _(g):
                    if _skip(7):
                        return
                    bwait(g, bar4b, G)
                    dma_wgu(g, 0)
                    dma_wgu(g, 1)
                    if _QCUT >= 0:
                        for rd in range(min(_QCUT, NRD)):
                            q_pool(g, 0, rd)
                        return
                    for rd in range(NRD):
                        q_pool(g, 0, rd)
                    for c in range(NCH):
                        if c + 2 < NCH:
                            dma_wgu(g, c + 2)
                        if c + 1 < NCH:
                            for rd in range(NRD):
                                q_pool(g, c + 1, rd)

                @block.tensor
                def _(t):
                    if _skip(7):
                        return
                    bwait(t, bar4b, S)
                    if _QCUT >= 0:
                        for rd in range(min(_QCUT, NRD)):
                            q_pe(t, 0, rd)
                        return
                    for rd in range(NRD):
                        q_pe(t, 0, rd)
                    for c in range(NCH):
                        b = c & 1
                        P.wait(t, pF_(c, NRD - 1))
                        for th in range(2):
                            if _segok(f"gate{th}"):
                                if th == 1:
                                    A.wait(t, aSeg(c, "silu0"))
                                elif c >= 1:
                                    A.wait(t, aSeg(c - 1, "silu1"))
                                if _NOGU:
                                    mm = t.nop()
                                else:
                                    for dc in range(16):
                                        rhs_ = (h16[:, TC * dc + 512 * th:TC * dc + 512 * (th + 1)]
                                                if _GUALT else
                                                xcffn[:, TC * dc + 512 * th:TC * dc + 512 * (th + 1)])
                                        mm = t.matmul(
                                            p_g[:], w16g[b][:, 256 * dc:256 * dc + 128],
                                            rhs_, start=(dc == 0), stop=(dc == 15))
                                if _DRAIN2 and not _NOGU:
                                    mm = t.matmul(p_cols[0:1, 0:1], ones_b[:, 0:1],
                                                  ones_b[:, 0:1], start=True, stop=True,
                                                  skip_group_check=True)
                                S.inc(mm)
                            if _segok(f"up{th}"):
                                if th == 1:
                                    V.wait(t, vSeg(c, "hm0"))
                                elif c >= 1:
                                    V.wait(t, vSeg(c - 1, "hm1"))
                                if _NOGU:
                                    mm = t.nop()
                                else:
                                    for dc in range(16):
                                        rhs_ = (h16[:, TC * dc + 512 * th:TC * dc + 512 * (th + 1)]
                                                if _GUALT else
                                                xcffn[:, TC * dc + 512 * th:TC * dc + 512 * (th + 1)])
                                        mm = t.matmul(
                                            p_u[:], w16g[b][:, 256 * dc + 128:256 * (dc + 1)],
                                            rhs_, start=(dc == 0), stop=(dc == 15))
                                if _DRAIN2 and not _NOGU:
                                    mm = t.matmul(p_cols[0:1, 1:2], ones_b[:, 0:1],
                                                  ones_b[:, 0:1], start=True, stop=True,
                                                  skip_group_check=True)
                                S.inc(mm)
                        if c + 1 < NCH:
                            for rd in range(NRD):
                                q_pe(t, c + 1, rd)
                        for th in range(2):
                            if not _segok(f"rms{th}"):
                                continue
                            A.wait(t, aSeg(c, f"sq{th}"))
                            if _NORMS:
                                mm = t.nop()
                            else:
                                mm = t.matmul(p_rms[th][0:1, :], ones_b[:, 0:1], hsq[th][:],
                                              start=(c == 0), stop=(c == NCH - 1),
                                              skip_group_check=True)
                            S.inc(mm)

                @block.scalar
                def _(s):
                    if _skip(7):
                        return
                    bwait(s, bar4b, A)
                    if _QCUT >= 0:
                        for rd in range(min(_QCUT, NRD)):
                            q_act(s, 0, rd)
                        return
                    for rd in range(NRD):
                        q_act(s, 0, rd)
                    for c in range(NCH):
                        for tag in A_ORD:
                            if tag.startswith("ev"):
                                rd = int(tag[2:])
                                if c + 1 < NCH:
                                    q_act(s, c + 1, rd)
                            elif tag.startswith("sg"):
                                pass
                            elif tag.startswith("silu"):
                                if not _segok(tag):
                                    continue
                                th = int(tag[4:])
                                S.wait(s, sSeg(c, f"gate{th}"))
                                if th == 1:
                                    V.wait(s, vSeg(c, "hm0"))
                                elif c >= 1:
                                    V.wait(s, vSeg(c - 1, "hm1"))
                                A.inc(s.nop() if _NOSILU else (
                                    s.copy(htmp[:], xcffn[:, 0:512]) if _SILUSB else
                                    s.copy(htmp[:], p_g[:]) if _SILUCPY else
                                    s.activation(htmp[:], p_g[:], AF.Silu)))
                            else:
                                if not _segok(tag):
                                    continue
                                th = int(tag[2:])
                                V.wait(s, vSeg(c, f"hm{th}"))
                                if c >= 1:
                                    S.wait(s, sSeg(c - 1, f"rms{th}"))
                                A.inc(s.nop() if _NOSQ else s.activation(
                                    hsq[th][:],
                                    h16[:, TC * c + 512 * th:TC * c + 512 * (th + 1)],
                                    AF.Square))

                @block.vector
                def _(v):
                    if _skip(7):
                        return
                    bwait(v, bar4b, V)
                    if _QCUT >= 0:
                        for rd in range(min(_QCUT, NRD)):
                            q_dve(v, 0, rd)
                        return
                    for rd in range(NRD):
                        q_dve(v, 0, rd)
                    for c in range(NCH):
                        for tag in V_ORD:
                            if tag.startswith("aw"):
                                rd = int(tag[2:])
                                if c + 1 < NCH:
                                    q_dve(v, c + 1, rd)
                            elif tag.startswith("hm"):
                                if not _segok(tag):
                                    continue
                                th = int(tag[2:])
                                A.wait(v, aSeg(c, f"silu{th}"))
                                S.wait(v, sSeg(c, f"up{th}"))
                                V.inc(v.nop() if _NOHM else v.tensor_tensor(
                                    h16[:, TC * c + 512 * th:TC * c + 512 * (th + 1)],
                                    htmp[:],
                                    xcffn[:, 0:512] if (_HMALT or _HMSB) else p_u[:],
                                    ALU.mult))

                assert _SEGN < 0 or NCH == 1
                if _QCUT < 0 and _SEGN < 0:
                    assert _skip(7) or (S.n == S0 + SPRE + SR4 * NCH - 2 * NRD)
                    assert _skip(7) or (A.n == A0 + APRE + AR4 * NCH - 2 * NRD)
                    assert _skip(7) or (V.n == V0 + VPRE + VR4 * NCH - 3 * NRD)
                    assert _skip(7) or (P.n == P0 + PPRE + PR4 * NCH - PR4)

                bar4c = snap()
                S0, V0, A0 = S.n, V.n, A.n

                @block.vector
                def _(v):
                    if _skip(8):
                        return
                    bwait(v, bar4c, V)
                    for th in range(2):
                        mm = v.tensor_scalar(rms_row[:1, 512 * th:512 * (th + 1)],
                                             p_rms[th][0:1, :], 1.0 / 8192, None, ALU.mult)
                    V.inc(mm)
                    A.wait(v, A0 + 1)
                    V.inc(v.reciprocal(rms_row[:], rms_sd[:]))
                    S.wait(v, S0 + 1)
                    V.inc(v.tensor_copy(rmscols[:], p_cols[:]))

                @block.scalar
                def _(s):
                    if _skip(8):
                        return
                    bwait(s, bar4c, A)
                    V.wait(s, V0 + 1)
                    A.inc(s.activation(rms_sd[:], rms_row[:], AF.Sqrt, bias=eps_sb[0:1, 1:2]))

                @block.tensor
                def _(t):
                    if _skip(8):
                        return
                    bwait(t, bar4c, S)
                    V.wait(t, V0 + 2)
                    for j in range(8):
                        mm = t.matmul(p_cols[:, j:j + 1], rms_row[:1, 128 * j:128 * (j + 1)],
                                      one1_f[:1, :], start=True, stop=True)
                    S.inc(mm)

                assert _skip(8) or (V.n == V0 + 3 and A.n == A0 + 1 and S.n == S0 + 1)

                if debug:
                    bardd = snap()

                    @block.sync
                    def _(sy):
                        if _skip(8):
                            return
                        bwait(sy, bardd)
                        Y.inc(sy.dma_start(out=dbg["dbg_xcffn"], in_=xcffn[:]))
                        Y.inc(sy.dma_start(out=dbg["dbg_h16"], in_=h16[:, 0:8 * TC]))
                        Y.inc(sy.dma_start(out=dbg["dbg_wf"], in_=w16g[1][:]))

            pxc.close()
            bar5 = snap()
            p5 = ExitStack()
            with p5:
                sb8 = mksb(p5)
                wp16 = [sb8(f"wp16_{i}", [128, 8192], BF16) for i in range(2)]
                stage5 = [sb8(f"stage5{i}", [128, 512], F32) for i in range(2)]
                fin = [sb8(f"fin{i}", [128, 512], F32) for i in range(2)]
                x_t = [sb8(f"x_t{i}", [128, 512], F32) for i in range(2)]
                rb_t = [sb8(f"rb_t{i}", [128, 512], F32) for i in range(2)]
                p_pj = [p5.enter_context(nc.psum_tensor(f"p_pj{i}", [128, 512], F32))
                        for i in range(2)]
                p_tp5 = [p5.enter_context(nc.psum_tensor(f"p_tp5{i}", [128, 256], F32))
                         for i in range(2)]
                p_gsum5 = p5.enter_context(nc.psum_tensor("p_gsum5", [4, 512], F32))
                p_T5 = p5.enter_context(nc.psum_tensor("p_T5", [128, 512], F32))
                p_sc5 = p5.enter_context(nc.psum_tensor("p_sc5", [128, 512], F32))

                NDC, NRD5 = 16, 16
                S0, V0, A0, P0, G0, Y0 = S.n, V.n, A.n, P.n, G.n, Y.n

                S_ORD5 = (["proj0", "proj1"]
                          + [f"tp{th}{j4}" for th in range(2) for j4 in range(4)]
                          + [f"{k}{rd}" for rd in range(NRD5) for k in ("gs", "ts")])
                A_ORD5 = (["st0", "st1"]
                          + [f"{k}{rd}" for rd in range(NRD5) for k in ("ev", "sg")])
                V_ORD5 = ([f"fn{th}{j4}" for th in range(2) for j4 in range(4)]
                          + [f"{k}{rd}" for rd in range(NRD5) for k in ("aw", "cm", "m")])
                P_ORD5 = [f"{k}{rd}" for rd in range(NRD5) for k in ("wh", "wf")]
                SO5, AO5, VO5, PO5 = map(_offsets, (S_ORD5, A_ORD5, V_ORD5, P_ORD5))
                SR5, AR5, VR5, PR5 = len(S_ORD5), len(A_ORD5), len(V_ORD5), len(P_ORD5)
                SP_ORD5 = [f"{k}{rd}" for rd in range(NRD5) for k in ("gs", "ts")]
                AP_ORD5 = [f"{k}{rd}" for rd in range(NRD5) for k in ("ev", "sg")]
                VP_ORD5 = [f"{k}{rd}" for rd in range(NRD5) for k in ("aw", "cm", "m")]
                SPO5, APO5, VPO5 = map(_offsets, (SP_ORD5, AP_ORD5, VP_ORD5))
                SPRE5, APRE5, VPRE5, PPRE5 = len(SP_ORD5), len(AP_ORD5), len(VP_ORD5), len(P_ORD5)

                def mkpos5(base0, pre_ofs, npre, seg_ofs, nseg):
                    def f(tag):
                        def g_(c, rd):
                            t_ = f"{tag}{rd}"
                            if c == 0:
                                return base0 + (pre_ofs[t_] if t_ in pre_ofs else seg_ofs[t_])
                            return base0 + npre + nseg * (c - 1) + seg_ofs[t_]
                        return g_
                    return f

                sPos5 = mkpos5(S0, SPO5, SPRE5, SO5, SR5)
                aPos5 = mkpos5(A0, APO5, APRE5, AO5, AR5)
                vPos5 = mkpos5(V0, VPO5, VPRE5, VO5, VR5)
                pPos5 = mkpos5(P0, PO5, PPRE5, PO5, PR5)
                sG5, sT5 = sPos5("gs"), sPos5("ts")
                aE5, aS5 = aPos5("ev"), aPos5("sg")
                vA5, vC5, vM5 = vPos5("aw"), vPos5("cm"), vPos5("m")
                pW5, pF5 = pPos5("wh"), pPos5("wf")

                def sSeg5(c, tag):
                    return S0 + SPRE5 + SR5 * c + SO5[tag]

                def aSeg5(c, tag):
                    return A0 + APRE5 + AR5 * c + AO5[tag]

                def vSeg5(c, tag):
                    return V0 + VPRE5 + VR5 * c + VO5[tag]

                def gD5(j):
                    return (Gwp[j % 2], 16 * (j // 2 + 1))

                q_pe5, q_act5, q_dve5, q_pool5 = make_quant(
                    wp16, 4, 128, NRD5,
                    (sG5, sT5, aE5, aS5, vA5, vC5, vM5, pW5, pF5, gD5, p_gsum5, p_T5, p_sc5))

                def dma_wp(g, j):
                    if j >= 2:
                        S.wait(g, sSeg5(j - 2, "proj1"))
                    Gwp[j % 2].inc(g.dma_start(
                        out=wp16[j % 2][:].rearrange("p (hc u) -> p hc u", u=128),
                        in_=wpT[:, 128 * j:128 * (j + 1)].rearrange("(hc p) u -> p hc u", p=128)))

                @block.gpsimd
                def _(g):
                    if _skip(9):
                        return
                    bwait(g, bar5, G)
                    dma_wp(g, 0)
                    dma_wp(g, 1)
                    for rd in range(NRD5):
                        q_pool5(g, 0, rd)
                    for c in range(NDC):
                        if c + 2 < NDC:
                            dma_wp(g, c + 2)
                        if c + 1 < NDC:
                            for rd in range(NRD5):
                                q_pool5(g, c + 1, rd)

                @block.sync
                def _(sy):
                    if _skip(9):
                        return
                    Y.wait(sy, bar5[4][1])
                    bwait(sy, bar5, Y)
                    for c in range(NDC):
                        for th in range(2):
                            if c >= 1:
                                V.wait(sy, vSeg5(c - 1, f"fn{th}3"))
                            Yx[th].inc(sy.dma_start(
                                out=x_t[th][:].rearrange("p (j4 u) -> p j4 u", u=128),
                                in_=xres[512 * th:512 * (th + 1), 128 * c:128 * (c + 1)]
                                .rearrange("(j4 p) u -> p j4 u", p=128)))
                            Yrb[th].inc(sy.dma_start(
                                out=rb_t[th][:].rearrange("p (j4 u) -> p j4 u", u=128),
                                in_=out_c[512 * th:512 * (th + 1), 128 * c:128 * (c + 1)]
                                .rearrange("(j4 p) u -> p j4 u", p=128)))
                        for th in range(2):
                            V.wait(sy, vSeg5(c, f"fn{th}3"))
                            Yout[th].inc(sy.dma_start(
                                out=out_c[512 * th:512 * (th + 1), 128 * c:128 * (c + 1)]
                                .rearrange("(j4 p) u -> p j4 u", p=128),
                                in_=fin[th][:].rearrange("p (j4 u) -> p j4 u", u=128)))
                    for ctr in (Yx[0], Yx[1], Yrb[0], Yrb[1], Yout[0], Yout[1]):
                        ctr.wait(sy, ctr.n)

                @block.tensor
                def _(t):
                    if _skip(9):
                        return
                    bwait(t, bar5, S)
                    for rd in range(NRD5):
                        q_pe5(t, 0, rd)
                    for c in range(NDC):
                        b = c & 1
                        P.wait(t, pF5(c, NRD5 - 1))
                        for th in range(2):
                            if c >= 1:
                                A.wait(t, aSeg5(c - 1, f"st{th}"))
                            for hc in range(64):
                                mm = t.matmul(p_pj[th][:], wp16[b][:, 128 * hc:128 * (hc + 1)],
                                              h16[:, TC * hc + 512 * th:TC * hc + 512 * (th + 1)],
                                              start=(hc == 0), stop=(hc == 63))
                            S.inc(mm)
                        for th in range(2):
                            A.wait(t, aSeg5(c, f"st{th}"))
                            for j4 in range(4):
                                k = 4 * th + j4
                                Q = 4 * c + k // 2
                                if Q >= 2:
                                    c2, q2 = divmod(Q - 2, 4)
                                    k2 = 2 * q2 + 1
                                    V.wait(t, vSeg5(c2, f"fn{k2 // 4}{k2 % 4}"))
                                S.inc(t.transpose(
                                    p_tp5[(k // 2) % 2][:, 128 * (k % 2):128 * (k % 2 + 1)],
                                    stage5[th][:, 128 * j4:128 * (j4 + 1)], ident32[:]))
                        if c + 1 < NDC:
                            for rd in range(NRD5):
                                q_pe5(t, c + 1, rd)

                @block.scalar
                def _(s):
                    if _skip(9):
                        return
                    bwait(s, bar5, A)
                    for rd in range(NRD5):
                        q_act5(s, 0, rd)
                    for c in range(NDC):
                        for tag in A_ORD5:
                            if tag.startswith("ev"):
                                rd = int(tag[2:])
                                if c + 1 < NDC:
                                    q_act5(s, c + 1, rd)
                            elif tag.startswith("sg"):
                                pass
                            else:
                                th = int(tag[2:])
                                S.wait(s, sSeg5(c, f"proj{th}"))
                                if c >= 1:
                                    S.wait(s, sSeg5(c - 1, f"tp{th}3"))
                                A.inc(s.copy(stage5[th][:], p_pj[th][:]))

                @block.vector
                def _(v):
                    if _skip(9):
                        return
                    bwait(v, bar5, V)
                    for rd in range(NRD5):
                        q_dve5(v, 0, rd)
                    for c in range(NDC):
                        for tag in V_ORD5:
                            if tag.startswith("aw"):
                                rd = int(tag[2:])
                                if c + 1 < NDC:
                                    q_dve5(v, c + 1, rd)
                            elif tag.startswith("fn"):
                                th, j4 = int(tag[2]), int(tag[3])
                                k = 4 * th + j4
                                kend = 2 * (k // 2) + 1
                                S.wait(v, sSeg5(c, f"tp{kend // 4}{kend % 4}"))
                                Yx[th].wait(v, 16 * (c + 1))
                                Yrb[th].wait(v, 16 * (c + 1))
                                if c >= 1:
                                    Yout[th].wait(v, 16 * c)
                                v.scalar_tensor_tensor(
                                    fin[th][:, 128 * j4:128 * (j4 + 1)],
                                    p_tp5[(k // 2) % 2][:, 128 * (k % 2):128 * (k % 2 + 1)],
                                    rmscols[:, 4 * th + j4:4 * th + j4 + 1],
                                    x_t[th][:, 128 * j4:128 * (j4 + 1)],
                                    ALU.mult, ALU.add)
                                V.inc(v.tensor_tensor(fin[th][:, 128 * j4:128 * (j4 + 1)],
                                                      fin[th][:, 128 * j4:128 * (j4 + 1)],
                                                      rb_t[th][:, 128 * j4:128 * (j4 + 1)],
                                                      ALU.add))

                assert _skip(9) or (S.n == S0 + SPRE5 + SR5 * NDC - 2 * NRD5)
                assert _skip(9) or (A.n == A0 + APRE5 + AR5 * NDC - 2 * NRD5)
                assert _skip(9) or (V.n == V0 + VPRE5 + VR5 * NDC - 3 * NRD5)
                assert _skip(9) or (P.n == P0 + PPRE5 + PR5 * NDC - PR5)
                assert _skip(9) or (Y.n == Y0 and Yout[0].n == 16 * NDC
                                    and Yout[1].n == 16 * NDC)

    return nc


_cached = {}


def _consts():
    sel2dup = np.zeros((128, 4), dtype=ml_dtypes.bfloat16)
    sel2dup[:64, 0] = 1
    sel2dup[:64, 1] = 1
    sel2dup[64:, 2] = 1
    sel2dup[64:, 3] = 1
    sel4 = np.zeros((4, 128), dtype=ml_dtypes.bfloat16)
    sel4[0, :64] = 1
    sel4[1, :64] = 1
    sel4[2, 64:] = 1
    sel4[3, 64:] = 1
    selsc = np.zeros((4, 128), dtype=ml_dtypes.bfloat16)
    selsc[0, :64] = 2
    selsc[2, 64:] = 2
    halfcol = np.array([[0.5], [2.0 ** -10], [0.5], [2.0 ** -10]], dtype=np.float32)
    return {
        "c_sel2dup": sel2dup, "c_sel4": sel4, "c_selsc": selsc, "c_halfcol": halfcol,
        "c_ones_b": np.ones((128, 1), ml_dtypes.bfloat16),
        "c_ones_f": np.ones((128, 1), np.float32),
        "c_onesrow_b": np.ones((1, 128), ml_dtypes.bfloat16),
        "c_onesrow_f": np.ones((1, 128), np.float32),
        "c_one1_f": np.ones((1, 1), np.float32),
        "c_ident16": np.eye(128, dtype=ml_dtypes.bfloat16),
        "c_ident32": np.eye(128, dtype=np.float32),
        "c_eps": np.ascontiguousarray(np.broadcast_to(
            np.array([EPS_LN, EPS_RMS], np.float32)[None, :], (128, 2))),
    }


def make_in_maps(L0_attn_x, L0_attn_Wq, L0_attn_Wk, L0_attn_Wv, L0_attn_Wo, L0_attn_bo,
                 L0_ffn_w_gate_up, L0_ffn_w_proj):
    x2d = np.ascontiguousarray(np.asarray(L0_attn_x, dtype=np.float32)[0])
    xT = np.ascontiguousarray(x2d.T)
    Wq = np.asarray(L0_attn_Wq, np.float32)
    Wk = np.asarray(L0_attn_Wk, np.float32)
    Wv = np.asarray(L0_attn_Wv, np.float32)
    Wo = np.ascontiguousarray(np.asarray(L0_attn_Wo, np.float32))
    bo = np.asarray(L0_attn_bo, np.float32)
    wguT = np.ascontiguousarray(np.asarray(L0_ffn_w_gate_up, np.float32).T)
    wpT = np.ascontiguousarray(np.asarray(L0_ffn_w_proj, np.float32).T)
    bo_b = np.ascontiguousarray(np.broadcast_to(bo[None, :], (128, D)))
    consts = _consts()
    in_maps = []
    for c in range(NCORE):
        m = {
            "xT": xT,
            "xres": np.ascontiguousarray(x2d[TC * c:TC * (c + 1)]),
            "xresT": np.ascontiguousarray(xT[:, TC * c:TC * (c + 1)]),
            "wq": np.ascontiguousarray(Wq[:, 256 * c:256 * (c + 1)]),
            "wk": np.ascontiguousarray(Wk[:, 256 * c:256 * (c + 1)]),
            "wv": np.ascontiguousarray(Wv[:, 256 * c:256 * (c + 1)]),
            "wo": Wo,
            "bo_b": bo_b,
            "wguT": wguT,
            "wpT": wpT,
        }
        m.update(consts)
        in_maps.append(m)
    return in_maps


def kernel(L0_attn_x, L0_attn_Wq, L0_attn_Wk, L0_attn_Wv, L0_attn_Wo, L0_attn_bo,
           L0_ffn_w_gate_up, L0_ffn_w_proj):
    in_maps = make_in_maps(L0_attn_x, L0_attn_Wq, L0_attn_Wk, L0_attn_Wv, L0_attn_Wo,
                           L0_attn_bo, L0_ffn_w_gate_up, L0_ffn_w_proj)
    if "nc" not in _cached:
        _cached["nc"] = build_program(debug=False)
    res = run_bass_kernel_spmd(_cached["nc"], in_maps, core_ids=list(range(NCORE)))
    out = np.concatenate([res.results[c]["out_c"] for c in range(NCORE)], axis=0)
    return out.reshape(1, T, D)
